# version 01 (final)
# baseline (speedup 1.0000x reference)
import numpy as np
import ml_dtypes
from contextlib import ExitStack

import concourse.bass as bass
import concourse.mybir as mybir
from concourse.bass_utils import run_bass_kernel_spmd

F32 = mybir.dt.float32
BF16 = mybir.dt.bfloat16
AF = mybir.ActivationFunctionType
ALU = mybir.AluOpType
AX = mybir.AxisListType

S_LEN = 2048
D = 1024
NH = 16
HD = 64
DH = 4096
TFF = 1024
NTILE = S_LEN // TFF
NTT = S_LEN // 128
EPS = 1e-5
SC_BASE = float(8.0 / np.log(65.0))

SBUF_BASE = 16512
OFF_R = 0
OFF_A = 65536
OFF_B = 131072
OFF_S = 196608
S_WSLOT = OFF_S + 0
S_SCALE = OFF_S + 8192
S_S3BC = OFF_S + 8704
S_SMALL = OFF_S + 10752
S_MISC = OFF_S + 11008
S_IDB = OFF_S + 11392
S_TRI = OFF_S + 11648
S_ONES1 = OFF_S + 11904
S_STAGE = OFF_S + 12416
S_PMAT = OFF_S + 14464
S_IDF = OFF_S + 14720
S_ONELO = OFF_S + 15232
S_PMATF = OFF_S + 15360


class Sem:
    def __init__(self, ctx, nc, name):
        self.h = ctx.enter_context(nc.semaphore(name))
        self.n = 0

    def inc(self, instr, k=1):
        instr.then_inc(self.h, k)
        self.n += k
        return self.n




_PRODUCERS = {"matmul", "transpose", "activation", "copy", "mul", "add", "square",
              "tensor_copy", "tensor_tensor", "tensor_add", "tensor_sub", "tensor_mul",
              "tensor_scalar", "tensor_single_scalar", "tensor_scalar_add",
              "tensor_scalar_sub", "tensor_scalar_mul", "scalar_tensor_tensor",
              "tensor_tensor_reduce", "reduce_sum", "reduce_max", "reciprocal",
              "reciprocal_approx_fast", "memset", "affine_select", "tensor_reduce"}


class Rec:

    def __init__(self, eng):
        self._e = eng
        self.last = None

    def __getattr__(self, name):
        f = getattr(self._e, name)
        if name in _PRODUCERS:
            def wrap(*a, **k):
                r = f(*a, **k)
                self.last = r
                return r
            return wrap
        return f


class MK:
    def __init__(self, nc, ctx, n_sub=3):
        self.nc = nc
        self.ctx = ctx
        self.n_sub = n_sub
        ent = ctx.enter_context

        dr = lambda name, shape, dt=F32: nc.dram_tensor(name, shape, dt, kind="ExternalInput").ap()
        self.x_d = dr("x", [S_LEN, D])
        self.w12p_d = dr("w12p", [32, 128, 2048])
        self.h1p_d = dr("h1p", [32, 128, 4096])
        self.h2p_d = dr("h2p", [32, 128, 4096])
        self.w3p_d = dr("w3p", [32, 128, 1024])
        self.s1m_d = dr("s1m", [128, 32])
        self.s2m_d = dr("s2m", [128, 32])
        self.h1m_d = dr("h1m", [128, 32])
        self.h2m_d = dr("h2m", [128, 32])
        self.s3bch_d = dr("s3bch", [128, D])
        self.wqp_d = dr("wqp", [8, 128, 1024])
        self.wkp_d = dr("wkp", [8, 128, 1024])
        self.wvp_d = dr("wvp", [8, 128, 1024])
        self.wop_d = dr("wop", [8, 128, 1024])
        self.alphacol_d = dr("alphacol", [128, 1])
        self.identb_d = dr("identb", [128, 128], BF16)
        self.identf_d = dr("identf", [128, 128])
        self.pmat_d = dr("pmat", [128, 128], BF16)
        self.pmatf_d = dr("pmatf", [128, 128])
        self.c128_d = dr("c128", [128, 2048], BF16)
        self.s128_d = dr("s128", [128, 2048], BF16)
        self.trib_d = dr("trib", [128, 128], BF16)
        self.oneslo_d = dr("oneslo", [128, 1])
        self.oneshi_d = dr("oneshi", [128, 1])
        self.ones33_d = dr("ones33", [33, 128])
        self.y_d = nc.dram_tensor("y", [S_LEN, D], F32, kind="ExternalOutput").ap()

        at = lambda name, shape, dt, offset: nc.alloc_sbuf_tensor_at(
            name, shape, dt, offset=offset + SBUF_BASE)
        self.r = at("r", [128, 16 * 1024], F32, offset=OFF_R)
        self.bufA = at("bufA", [128, 32 * 1024], BF16, offset=OFF_A)
        self.bufB = at("bufB", [128, 32 * 1024], BF16, offset=OFF_B)
        self.u_fm = at("u_fm", [128, 8 * 1024], BF16, offset=OFF_B)
        self.t1 = at("t1", [128, 2 * 1024], BF16, offset=OFF_B + 16384)
        self.u_att = at("u_att", [128, 8 * 2048], BF16, offset=OFF_A)
        self.av_fm = at("av_fm", [128, 8 * 2048], BF16, offset=OFF_A + 32768)
        self.q2_sb = at("q2_sb", [128, 2048], BF16, offset=OFF_B + 0)
        self.kr_sb = at("kr_sb", [128, 2048], BF16, offset=OFF_B + 4096)
        self.vtm_sb = at("vtm_sb", [128, 2048], BF16, offset=OFF_B + 8192)
        self.num_sb = at("num_sb", [128, 2048], F32, offset=OFF_B + 12288)
        self.vfm_sb = at("vfm_sb", [128, 2048], BF16, offset=OFF_B + 12288)
        self.den_sb = at("den_sb", [128, 2048], F32, offset=OFF_B + 20480)
        self.rcp_sb = at("rcp_sb", [128, 2048], F32, offset=OFF_B + 28672)
        self.e_sb = at("e_sb", [128, 2048], BF16, offset=OFF_B + 36864)
        self.qcst = at("qcst", [128, 2048], BF16, offset=OFF_B + 36864)
        self.qsst = at("qsst", [128, 2048], BF16, offset=OFF_B + 40960)
        self.eT_sb = at("eT_sb", [128, 2048], BF16, offset=OFF_B + 45056)
        self.qcF = at("qcF", [128, 512], F32, offset=OFF_B + 36864)
        self.qsF = at("qsF", [128, 512], F32, offset=OFF_B + 45056)
        self.c128_s = at("c128_s", [128, 2048], BF16, offset=OFF_B + 49152)
        self.s128_s = at("s128_s", [128, 2048], BF16, offset=OFF_B + 53248)
        self.knrowF = at("knrowF", [33, 2048], F32, offset=OFF_B + 57344)
        self.wo_sb = at("wo_sb", [128, 8 * 1024], BF16, offset=OFF_B + 0)
        self.wslot = at("wslot", [128, 2 * 2048], BF16, offset=S_WSLOT)
        self.s1m = at("s1m", [128, 32], F32, offset=S_SCALE)
        self.s2m = at("s2m", [128, 32], F32, offset=S_SCALE + 128)
        self.h1m = at("h1m", [128, 32], F32, offset=S_SCALE + 256)
        self.h2m = at("h2m", [128, 32], F32, offset=S_SCALE + 384)
        self.s3bc = at("s3bc", [128, 1024], BF16, offset=S_S3BC)
        self.smallc = at("smallc", [128, 64], F32, offset=S_SMALL)
        self.alphacol = at("alphacol", [128, 1], F32, offset=S_MISC)
        self.oneslo = at("oneslo", [128, 1], F32, offset=S_ONELO)
        self.oneshi = at("oneshi", [128, 1], F32, offset=S_ONELO + 64)
        self.identb = at("identb", [128, 128], BF16, offset=S_IDB)
        self.trib = at("trib", [128, 128], BF16, offset=S_TRI)
        self.ones33 = at("ones33", [33, 128], F32, offset=S_ONES1)
        self.stage = at("stage", [128, 1024], BF16, offset=S_STAGE)
        self.pmat_s = at("pmat_s", [128, 128], BF16, offset=S_PMAT)
        self.identf_s = at("identf_s", [128, 128], F32, offset=S_IDF)
        self.pmatf_s = at("pmatf_s", [128, 128], F32, offset=S_PMATF)

        self.C_NSUM, self.C_SQS, self.C_NMEAN, self.C_M2 = 0, 1, 2, 3
        self.C_VAR, self.C_VARE, self.C_LNV, self.C_RSTD = 4, 5, 6, 7
        self.C_SQSCALE = 8
        self.C_MAX, self.C_NEGM, self.C_Z, self.C_ZINV = 9, 10, 11, 12
        self.C_QNE = 16
        self.C_QNO = 32

        self.x = Sem(ctx, nc, "sx")
        self.w = Sem(ctx, nc, "sw")
        self.wfree = Sem(ctx, nc, "swf")
        self.pe = Sem(ctx, nc, "spe")
        self.ve = Sem(ctx, nc, "sve")
        self.sc = Sem(ctx, nc, "ssc")
        self.st = Sem(ctx, nc, "sst")
        self.ring_i = 0
        self.ring_gate = {}
        self.uid = 0

        self.block = ctx.enter_context(nc.Block())

    def uname(self, base):
        self.uid += 1
        return f"{base}{self.uid}"

    def join(self):
        blk = self.block
        pe_n, ve_n, sc_n = self.pe.n, self.ve.n, self.sc.n

        @blk.tensor
        def _(t):
            t = Rec(t)
            t.wait_ge(self.ve.h, ve_n)
            t.wait_ge(self.sc.h, sc_n)

        @blk.vector
        def _(v):
            v = Rec(v)
            v.wait_ge(self.pe.h, pe_n)
            v.wait_ge(self.sc.h, sc_n)

        @blk.scalar
        def _(c):
            c = Rec(c)
            c.wait_ge(self.pe.h, pe_n)
            c.wait_ge(self.ve.h, ve_n)

    def ring_dma(self, g, src_ap, cols=2048):
        i = self.ring_i
        slot = i % 2
        if i >= 2:
            gate = self.ring_gate[i - 2]
            g.wait_ge(self.pe.h, gate)
        dst = self.wslot[:, slot * 2048: slot * 2048 + cols]
        w_val = self.w.inc(g.dma_start(out=dst, in_=src_ap), 16)
        self.ring_i += 1
        return slot, w_val

    def consume_panel(self, idx, pe_count=None):
        self.ring_gate[idx] = self.pe.n if pe_count is None else pe_count

    def preamble(self):
        blk = self.block

        @blk.sync
        def _(s):
            xin = self.x_d.rearrange("(tt p) d -> p tt d", p=128)
            rout = self.r[:].rearrange("p (tt d) -> p tt d", tt=16)
            self.x.inc(s.dma_start(out=rout, in_=xin), 16)
            for dst, src in [
                (self.s1m, self.s1m_d), (self.s2m, self.s2m_d),
                (self.h1m, self.h1m_d), (self.h2m, self.h2m_d),
                (self.identb, self.identb_d), (self.trib, self.trib_d),
                (self.oneslo, self.oneslo_d), (self.oneshi, self.oneshi_d),
                (self.ones33, self.ones33_d), (self.pmat_s, self.pmat_d),
                (self.identf_s, self.identf_d), (self.pmatf_s, self.pmatf_d),
                (self.alphacol, self.alphacol_d),
            ]:
                self.x.inc(s.dma_start(out=dst[:], in_=src), 16)
        x_n = self.x.n

        @blk.gpsimd
        def _(g):
            self.w.inc(g.dma_start(out=self.s3bc[:], in_=self.s3bch_d), 16)
        s3_w = self.w.n
        self.s3_w = s3_w

        @blk.scalar
        def _(c):
            c = Rec(c)
            c.wait_ge(self.x.h, x_n)
            self.sc.inc(c.activation(self.smallc[:, self.C_SQSCALE:self.C_SQSCALE + 1],
                         self.alphacol[:], AF.Exp, scale=0.5 * float(np.log(SC_BASE))))
            c.wait_ge(self.sc.h, self.sc.n)
            c.mul(self.smallc[:, self.C_SQSCALE:self.C_SQSCALE + 1],
                  self.smallc[:, self.C_SQSCALE:self.C_SQSCALE + 1], 0.5)
            self.sc.inc(c.last)
        self.join()

    def emit_ln(self, t0, ntt, dst, dst_chunk_cols, col_off):
        blk = self.block
        sm = self.smallc
        with ExitStack() as pctx:
            ptr = [pctx.enter_context(self.nc.psum_tensor(self.uname("pln"), [128, 128], F32))
                   for _ in range(8)]
            ve_evac = 0
            pe_tr = 0
            var_ve_prev = 0
            for j in range(ntt):
                rsl = self.r[:, (t0 + j) * 1024:(t0 + j + 1) * 1024]

                @blk.vector
                def _(v, rsl=rsl):
                    v = Rec(v)
                    v.reduce_sum(sm[:, 0:1], rsl, axis=AX.X, negate=True)
                    self.ve.inc(v.last)
                ve_n = self.ve.n

                @blk.scalar
                def _(c, rsl=rsl, pe_need=pe_tr, ve_n=ve_n, var_ve_prev=var_ve_prev):
                    c = Rec(c)
                    c.wait_ge(self.pe.h, pe_need)
                    c.wait_ge(self.ve.h, ve_n)
                    c.wait_ge(self.ve.h, var_ve_prev)
                    c.activation(self.stage[:], rsl, AF.Square,
                                 accum_out=sm[:, 1:2])
                    self.sc.inc(c.last)
                sq_sc = self.sc.n

                @blk.vector
                def _(v, sq_sc=sq_sc, ve_n=ve_n):
                    v = Rec(v)
                    v.wait_ge(self.ve.h, ve_n)
                    self.ve.inc(v.tensor_scalar_mul(sm[:, 2:3], sm[:, 0:1], 1.0 / 1024.0))
                    v.wait_ge(self.ve.h, self.ve.n)
                    self.ve.inc(v.tensor_tensor(sm[:, 3:4], sm[:, 2:3], sm[:, 2:3], op=ALU.mult))
                    v.wait_ge(self.ve.h, self.ve.n)
                    v.wait_ge(self.sc.h, sq_sc)
                    self.ve.inc(v.scalar_tensor_tensor(sm[:, 4:5], sm[:, 1:2], 1.0 / 1024.0,
                                                       sm[:, 3:4], ALU.mult, ALU.subtract))
                    v.wait_ge(self.ve.h, self.ve.n)
                    v.tensor_scalar_add(sm[:, 5:6], sm[:, 4:5], EPS)
                    self.ve.inc(v.last)
                var_ve = self.ve.n
                var_ve_prev = var_ve

                @blk.scalar
                def _(c, var_ve=var_ve):
                    c = Rec(c)
                    c.wait_ge(self.ve.h, var_ve)
                    self.sc.inc(c.activation(sm[:, 6:7], sm[:, 5:6], AF.Ln))
                    c.wait_ge(self.sc.h, self.sc.n)
                    c.activation(sm[:, 7:8], sm[:, 6:7], AF.Exp, scale=-0.5)
                    self.sc.inc(c.last)
                rstd_sc = self.sc.n

                @blk.vector
                def _(v, rsl=rsl, rstd_sc=rstd_sc, var_ve=var_ve):
                    v = Rec(v)
                    v.wait_ge(self.ve.h, var_ve)
                    v.wait_ge(self.sc.h, rstd_sc)
                    v.tensor_scalar(self.stage[:], rsl, sm[:, 2:3], sm[:, 7:8],
                                    ALU.add, ALU.mult)
                    self.ve.inc(v.last)
                norm_ve = self.ve.n

                @blk.tensor
                def _(t, norm_ve=norm_ve, ve_evac=ve_evac):
                    t = Rec(t)
                    t.wait_ge(self.ve.h, norm_ve)
                    t.wait_ge(self.ve.h, ve_evac)
                    for k in range(8):
                        t.matmul(ptr[k][:], self.stage[:, k * 128:(k + 1) * 128],
                                 self.identb[:], start=True, stop=True)
                    self.pe.inc(t.last)
                pe_tr = self.pe.n
                tr_pe = self.pe.n

                @blk.vector
                def _(v, j=j, tr_pe=tr_pe):
                    v = Rec(v)
                    v.wait_ge(self.pe.h, tr_pe)
                    for k in range(8):
                        self.ve.inc(v.tensor_copy(
                            dst[:, k * dst_chunk_cols + col_off + j * 128:
                                k * dst_chunk_cols + col_off + (j + 1) * 128],
                            ptr[k][:]))
                ve_evac = self.ve.n
        self.join()

    def emit_w12(self, tt):
        blk = self.block
        w_vals = []
        base_i = self.ring_i

        @blk.gpsimd
        def _(g):
            for oc in range(2):
                _, wv = self.ring_dma(g, self.w12p_d[oc])
                w_vals.append(wv)

        with ExitStack() as pctx:
            p1 = [pctx.enter_context(self.nc.psum_tensor(self.uname("p12"), [128, 1024], F32))
                  for _ in range(2)]
            p2 = [pctx.enter_context(self.nc.psum_tensor(self.uname("p12"), [128, 1024], F32))
                  for _ in range(2)]
            free_ve = {}
            for oc in range(32):
                half = oc % 2
                slot = (base_i + oc) % 2
                need_ve = free_ve.get(half, 0)

                @blk.tensor
                def _(t, oc=oc, half=half, slot=slot, need_ve=need_ve):
                    t = Rec(t)
                    t.wait_ge(self.w.h, w_vals[oc])
                    t.wait_ge(self.ve.h, need_ve)
                    for si, ps in ((0, p1[half]), (1, p2[half])):
                        for kc in range(8):
                            lhsT = self.wslot[:, slot * 2048 + si * 1024 + kc * 128:
                                              slot * 2048 + si * 1024 + (kc + 1) * 128]
                            for ch in range(2):
                                t.matmul(ps[:, ch * 512:(ch + 1) * 512],
                                         lhsT,
                                         self.u_fm[:, kc * 1024 + ch * 512: kc * 1024 + (ch + 1) * 512],
                                         start=(kc == 0), stop=(kc == 7))
                    self.pe.inc(t.last)
                mm_pe = self.pe.n
                self.consume_panel(base_i + oc, mm_pe)
                if oc + 2 < 32:
                    @blk.gpsimd
                    def _(g, oc=oc):
                        _, wv = self.ring_dma(g, self.w12p_d[oc + 2])
                        w_vals.append(wv)

                @blk.scalar
                def _(c, oc=oc, half=half, mm_pe=mm_pe, need_ve=need_ve):
                    c = Rec(c)
                    c.wait_ge(self.pe.h, mm_pe)
                    c.wait_ge(self.ve.h, need_ve)
                    c.activation(self.t1[:, half * 1024:(half + 1) * 1024],
                                 p1[half][:], AF.Silu,
                                 scale=self.s1m[:, oc:oc + 1])
                    self.sc.inc(c.last)
                silu_sc = self.sc.n

                @blk.vector
                def _(v, oc=oc, half=half, silu_sc=silu_sc):
                    v = Rec(v)
                    v.wait_ge(self.sc.h, silu_sc)
                    v.scalar_tensor_tensor(
                        self.bufA[:, oc * 1024:(oc + 1) * 1024],
                        p2[half][:],
                        self.s2m[:, oc:oc + 1],
                        self.t1[:, half * 1024:(half + 1) * 1024],
                        ALU.mult, ALU.mult)
                    self.ve.inc(v.last)
                free_ve[half] = self.ve.n
        self.join()

    def emit_h(self, tt, src, dst, panels_d, scale_m):
        blk = self.block
        w_vals = []
        base_i = self.ring_i

        @blk.gpsimd
        def _(g):
            for hf in range(2):
                _, wv = self.ring_dma(g, panels_d[0][:, hf * 2048:(hf + 1) * 2048])
                w_vals.append(wv)

        with ExitStack() as pctx:
            ph = [pctx.enter_context(self.nc.psum_tensor(self.uname("ph"), [128, 1024], F32))
                  for _ in range(2)]
            free_sc = {}
            for oc in range(32):
                half = oc % 2
                need_sc = free_sc.get(half, 0)
                base = base_i + oc * 2

                @blk.tensor
                def _(t, oc=oc, half=half, base=base, need_sc=need_sc):
                    t = Rec(t)
                    t.wait_ge(self.sc.h, need_sc)
                    for hf in range(2):
                        t.wait_ge(self.w.h, w_vals[oc * 2 + hf])
                        slot = (base + hf) % 2
                        for kci in range(16):
                            kc = hf * 16 + kci
                            lhsT = self.wslot[:, slot * 2048 + kci * 128:
                                              slot * 2048 + (kci + 1) * 128]
                            for ch in range(2):
                                t.matmul(ph[half][:, ch * 512:(ch + 1) * 512],
                                         lhsT,
                                         src[:, kc * 1024 + ch * 512: kc * 1024 + (ch + 1) * 512],
                                         start=(kc == 0), stop=(kc == 31))
                    self.pe.inc(t.last)
                mm_pe = self.pe.n
                self.consume_panel(base_i + oc * 2, mm_pe)
                self.consume_panel(base_i + oc * 2 + 1, mm_pe)
                if oc + 1 < 32:
                    @blk.gpsimd
                    def _(g, oc=oc):
                        for hf in range(2):
                            _, wv = self.ring_dma(g, panels_d[oc + 1][:, hf * 2048:(hf + 1) * 2048])
                            w_vals.append(wv)

                @blk.scalar
                def _(c, oc=oc, half=half, mm_pe=mm_pe):
                    c = Rec(c)
                    c.wait_ge(self.pe.h, mm_pe)
                    c.activation(dst[:, oc * 1024:(oc + 1) * 1024],
                                 ph[half][:], AF.Gelu,
                                 scale=scale_m[:, oc:oc + 1])
                    self.sc.inc(c.last)
                free_sc[half] = self.sc.n
        self.join()

    def emit_w3(self, tt):
        blk = self.block
        w_vals = []
        h2_done_pe = self.pe.n

        @blk.gpsimd
        def _(g):
            for hc in range(32):
                if hc < 2:
                    _, wv = self.ring_dma(g, self.w3p_d[hc], cols=1024)
                else:
                    g.wait_ge(self.pe.h, h2_done_pe)
                    wv = self.w.inc(g.dma_start(
                        out=self.bufB[:, hc * 1024:(hc + 1) * 1024],
                        in_=self.w3p_d[hc]), 16)
                w_vals.append(wv)

        chunk_ap = []
        for hc in range(32):
            if hc < 2:
                slot = (self.ring_i - 2 + hc) % 2
                chunk_ap.append(self.wslot[:, slot * 2048: slot * 2048 + 1024])
            else:
                chunk_ap.append(self.bufB[:, hc * 1024:(hc + 1) * 1024])
        ve_scaled = []

        @blk.vector
        def _(v):
            v = Rec(v)
            for hc in range(32):
                v.wait_ge(self.w.h, w_vals[hc])
                self.ve.inc(v.tensor_tensor(self.stage[:], chunk_ap[hc], self.s3bc[:],
                                            op=ALU.mult))
                v.wait_ge(self.ve.h, self.ve.n)
                v.tensor_copy(chunk_ap[hc], self.stage[:])
                ve_scaled.append(self.ve.inc(v.last))
                v.wait_ge(self.ve.h, self.ve.n)

        with ExitStack() as pctx:
            py = [pctx.enter_context(self.nc.psum_tensor(self.uname("py"), [128, 1024], F32))
                  for _ in range(4)]
            prev_add_ve = self.ve.n
            for grp in range(2):
                @blk.tensor
                def _(t, grp=grp, grp_free_ve=prev_add_ve):
                    t = Rec(t)
                    if grp == 1:
                        t.wait_ge(self.ve.h, grp_free_ve)
                    for hc in range(32):
                        t.wait_ge(self.ve.h, ve_scaled[hc])
                        for tjl in range(4):
                            tj = grp * 4 + tjl
                            lhsT = self.bufA[:, hc * 1024 + tj * 128: hc * 1024 + (tj + 1) * 128]
                            for ch in range(2):
                                t.matmul(py[tjl][:, ch * 512:(ch + 1) * 512],
                                         lhsT, chunk_ap[hc][:, ch * 512:(ch + 1) * 512],
                                         start=(hc == 0), stop=(hc == 31))
                    self.pe.inc(t.last)
                mm_pe = self.pe.n
                if grp == 1:
                    self.consume_panel(self.ring_i - 2, mm_pe)
                    self.consume_panel(self.ring_i - 1, mm_pe)

                @blk.vector
                def _(v, grp=grp, mm_pe=mm_pe):
                    v = Rec(v)
                    v.wait_ge(self.pe.h, mm_pe)
                    for tjl in range(4):
                        tj = grp * 4 + tjl
                        rsl = self.r[:, (tt * 8 + tj) * 1024:(tt * 8 + tj + 1) * 1024]
                        self.ve.inc(v.scalar_tensor_tensor(rsl, py[tjl][:], 0.5, rsl,
                                                           ALU.mult, ALU.add))
                prev_add_ve = self.ve.n
        self.join()

    def emit_ffn(self, app):
        for tt in range(NTILE):
            self.emit_ln(tt * 8, 8, self.u_fm, 1024, 0)
            self.emit_w12(tt)
            self.emit_h(tt, self.bufA, self.bufB, self.h1p_d, self.h1m)
            self.emit_h(tt, self.bufB, self.bufA, self.h2p_d, self.h2m)
            self.emit_w3(tt)

    def att_pair_setup(self, hp):
        blk = self.block
        nc = self.nc
        sm = self.smallc

        w_vals = []
        slot_base = self.ring_i

        @blk.gpsimd
        def _(g):
            for wd in (self.wqp_d, self.wkp_d):
                _, wv = self.ring_dma(g, wd[hp][:, 0:1024], cols=1024)
                w_vals.append(wv)

        @blk.vector
        def _(v):
            v = Rec(v)
            self.ve.inc(v.memset(self.den_sb[:], 0.0))
        ms_ve = self.ve.n

        with ExitStack() as pctx:
            pp = [pctx.enter_context(nc.psum_tensor(self.uname("pp"), [128, 512], F32))
                  for _ in range(4)]
            for ti, tgt in enumerate(("q", "k")):
                slot = (slot_base + ti) % 2
                wv = w_vals[ti]

                @blk.tensor
                def _(t, slot=slot, wv=wv, ti=ti):
                    t = Rec(t)
                    t.wait_ge(self.w.h, wv)
                    t.wait_ge(self.x.h, self.x.n)
                    if ti > 0:
                        t.wait_ge(self.sc.h, self.sc.n)
                        t.wait_ge(self.ve.h, self.ve.n)
                    for ch in range(4):
                        for dc in range(8):
                            lhsT = self.wslot[:, slot * 2048 + dc * 128: slot * 2048 + (dc + 1) * 128]
                            t.matmul(pp[ch][:], lhsT,
                                     self.u_att[:, dc * 2048 + ch * 512: dc * 2048 + (ch + 1) * 512],
                                     start=(dc == 0), stop=(dc == 7))
                    self.pe.inc(t.last)
                proj_pe = self.pe.n
                self.consume_panel(slot_base + ti, proj_pe)
                if ti == 0:
                    @blk.gpsimd
                    def _(g):
                        _, wv = self.ring_dma(g, self.wvp_d[hp][:, 0:1024], cols=1024)
                        w_vals.append(wv)

                rope_used = 0
                for ch in range(4):
                    @blk.vector
                    def _(v, ch=ch, proj_pe=proj_pe, rope_used=rope_used):
                        v = Rec(v)
                        v.wait_ge(self.pe.h, proj_pe)
                        if rope_used:
                            v.wait_ge(self.pe.h, rope_used)
                        self.ve.inc(v.tensor_tensor(
                            self.qcF[:], pp[ch][:],
                            self.c128_s[:, ch * 512:(ch + 1) * 512], op=ALU.mult))
                        self.ve.inc(v.tensor_tensor(
                            self.qsF[:], pp[ch][:],
                            self.s128_s[:, ch * 512:(ch + 1) * 512], op=ALU.mult))
                    cs_ve = self.ve.n

                    @blk.tensor
                    def _(t, ch=ch, cs_ve=cs_ve):
                        t = Rec(t)
                        t.wait_ge(self.ve.h, cs_ve)
                        t.matmul(pp[ch][:], self.pmatf_s[:], self.qsF[:],
                                 start=True, stop=False)
                        t.matmul(pp[ch][:], self.identf_s[:], self.qcF[:],
                                 start=False, stop=True)
                        self.pe.inc(t.last)
                    rope_used = self.pe.n
                rope_pe = self.pe.n

                @blk.scalar
                def _(c, ti=ti, rope_pe=rope_pe):
                    c = Rec(c)
                    c.wait_ge(self.pe.h, rope_pe)
                    for ch in range(4):
                        cols = slice(ch * 512, (ch + 1) * 512)
                        if ti == 0:
                            self.sc.inc(c.activation(self.q2_sb[:, cols], pp[ch][:],
                                                     AF.Copy, scale=-2.0))
                        else:
                            self.sc.inc(c.activation(self.kr_sb[:, cols], pp[ch][:], AF.Copy))
                    c.wait_ge(self.sc.h, self.sc.n)
                    if ti == 0:
                        c.activation(self.num_sb[:], self.q2_sb[:], AF.Square, scale=-0.5)
                    else:
                        c.activation(self.num_sb[:], self.kr_sb[:], AF.Square)
                    self.sc.inc(c.last)
                sq_sc = self.sc.n

                @blk.tensor
                def _(t, sq_sc=sq_sc):
                    t = Rec(t)
                    t.wait_ge(self.sc.h, sq_sc)
                    for ch in range(4):
                        t.matmul(pp[ch][0:1, :], self.oneslo[:],
                                 self.num_sb[:, ch * 512:(ch + 1) * 512],
                                 start=True, stop=True)
                    self.pe.inc(t.last)
                qne_pe = self.pe.n

                @blk.scalar
                def _(c, ti=ti, qne_pe=qne_pe):
                    c = Rec(c)
                    c.wait_ge(self.pe.h, qne_pe)
                    if ti == 0:
                        c.wait_ge(self.ve.h, ms_ve)
                    dstrows = self.den_sb if ti == 0 else self.knrowF
                    for ch in range(4):
                        self.sc.inc(c.copy(dstrows[0:1, ch * 512:(ch + 1) * 512],
                                           pp[ch][0:1, :]))
                qne_sc = self.sc.n

                @blk.tensor
                def _(t, qne_sc=qne_sc):
                    t = Rec(t)
                    t.wait_ge(self.sc.h, qne_sc)
                    for ch in range(4):
                        t.matmul(pp[ch][32:33, :], self.oneshi[:],
                                 self.num_sb[:, ch * 512:(ch + 1) * 512],
                                 start=True, stop=True)
                    self.pe.inc(t.last)
                qno_pe = self.pe.n

                @blk.scalar
                def _(c, ti=ti, qno_pe=qno_pe):
                    c = Rec(c)
                    c.wait_ge(self.pe.h, qno_pe)
                    dstrows = self.den_sb if ti == 0 else self.knrowF
                    for ch in range(4):
                        self.sc.inc(c.copy(dstrows[32:33, ch * 512:(ch + 1) * 512],
                                           pp[ch][32:33, :]))
                rows_sc = self.sc.n

                if ti == 0:
                    prev_ve = self.ve.n
                    for qg in range(4):
                        @blk.tensor
                        def _(t, qg=qg, rows_sc=rows_sc, prev_ve=prev_ve):
                            t = Rec(t)
                            t.wait_ge(self.sc.h, rows_sc)
                            t.wait_ge(self.ve.h, prev_ve)
                            for i in range(4):
                                qt = qg * 4 + i
                                t.matmul(pp[i][:, 0:128],
                                         self.den_sb[:, qt * 128:(qt + 1) * 128],
                                         self.identf_s[:], start=True, stop=True)
                            self.pe.inc(t.last)
                        qtr_pe = self.pe.n

                        @blk.vector
                        def _(v, qg=qg, qtr_pe=qtr_pe):
                            v = Rec(v)
                            v.wait_ge(self.pe.h, qtr_pe)
                            for i in range(4):
                                qt = qg * 4 + i
                                self.ve.inc(v.tensor_copy(
                                    sm[:, self.C_QNE + qt:self.C_QNE + qt + 1],
                                    pp[i][:, 0:1]))
                                self.ve.inc(v.tensor_copy(
                                    sm[:, self.C_QNO + qt:self.C_QNO + qt + 1],
                                    pp[i][:, 32:33]))
                        prev_ve = self.ve.n

                    @blk.vector
                    def _(v, prev_ve=prev_ve):
                        v = Rec(v)
                        v.wait_ge(self.ve.h, prev_ve)
                        self.ve.inc(v.tensor_scalar_add(sm[:, self.C_QNE:self.C_QNE + 16],
                                                        sm[:, self.C_QNE:self.C_QNE + 16], EPS))
                        self.ve.inc(v.tensor_scalar_add(sm[:, self.C_QNO:self.C_QNO + 16],
                                                        sm[:, self.C_QNO:self.C_QNO + 16], EPS))

            slot = (slot_base + 2) % 2

            @blk.tensor
            def _(t, slot=slot):
                t = Rec(t)
                t.wait_ge(self.w.h, w_vals[2])
                t.wait_ge(self.sc.h, self.sc.n)
                t.wait_ge(self.ve.h, self.ve.n)
                for ch in range(4):
                    for dc in range(8):
                        lhsT = self.wslot[:, slot * 2048 + dc * 128: slot * 2048 + (dc + 1) * 128]
                        t.matmul(pp[ch][:], lhsT,
                                 self.u_att[:, dc * 2048 + ch * 512: dc * 2048 + (ch + 1) * 512],
                                 start=(dc == 0), stop=(dc == 7))
                self.pe.inc(t.last)
            vproj_pe = self.pe.n
            self.consume_panel(slot_base + 2, vproj_pe)

            @blk.scalar
            def _(c, vproj_pe=vproj_pe):
                c = Rec(c)
                c.wait_ge(self.pe.h, vproj_pe)
                for ch in range(4):
                    self.sc.inc(c.activation(self.vfm_sb[:, ch * 512:(ch + 1) * 512],
                                             pp[ch][:], AF.Copy))
            vfm_sc = self.sc.n

            prev_ve = self.ve.n
            for vg in range(4):
                @blk.tensor
                def _(t, vg=vg, vfm_sc=vfm_sc, prev_ve=prev_ve):
                    t = Rec(t)
                    t.wait_ge(self.sc.h, vfm_sc)
                    t.wait_ge(self.ve.h, prev_ve)
                    for i in range(4):
                        kt = vg * 4 + i
                        t.matmul(pp[i][:, 0:128],
                                 self.vfm_sb[:, kt * 128:(kt + 1) * 128],
                                 self.identb[:], start=True, stop=True)
                    self.pe.inc(t.last)
                vtr_pe = self.pe.n

                @blk.vector
                def _(v, vg=vg, vtr_pe=vtr_pe):
                    v = Rec(v)
                    v.wait_ge(self.pe.h, vtr_pe)
                    for i in range(4):
                        kt = vg * 4 + i
                        self.ve.inc(v.tensor_copy(self.vtm_sb[:, kt * 128:(kt + 1) * 128],
                                                  pp[i][:, 0:128]))
                prev_ve = self.ve.n
        self.join()

    def att_pair_strips(self, hp):
        blk = self.block
        sm = self.smallc
        nc = self.nc
        with ExitStack() as pctx:
            pss = [pctx.enter_context(nc.psum_tensor(self.uname("pss"), [128, 512], F32))
                   for _ in range(3)]
            pet = [pctx.enter_context(nc.psum_tensor(self.uname("pet"), [128, 128], F32))
                   for _ in range(3)]
            pav = [pctx.enter_context(nc.psum_tensor(self.uname("pav"), [64, 128], F32))
                   for _ in range(2)]
            ps_free = {}
            e_free_pe = 0
            et_free = {}
            slot_free = {}
            pav_free_sc = 0
            num_free = 0
            for qt in range(16):
                for hh in range(2):
                    W = (qt + 1) * 128
                    nch = (W + 511) // 512
                    qoff = (self.C_QNE if hh == 0 else self.C_QNO) + qt
                    qn_col = sm[:, qoff:qoff + 1]
                    sq_scs = {}
                    kn_pes = {}
                    for c_i in range(nch + 1):
                        if c_i < nch:
                            wc = min(512, W - c_i * 512)
                            bank = c_i % 3
                            need_ve = ps_free.get(bank, 0)
                            pslice = pss[bank][:, 0:wc]

                            @blk.tensor
                            def _(t, hh=hh, qt=qt, c_i=c_i, wc=wc, pslice=pslice,
                                  need_ve=need_ve):
                                t = Rec(t)
                                t.wait_ge(self.ve.h, need_ve)
                                t.matmul(pslice,
                                         self.q2_sb[hh * 64:(hh + 1) * 64,
                                                    qt * 128:(qt + 1) * 128],
                                         self.kr_sb[hh * 64:(hh + 1) * 64,
                                                    c_i * 512: c_i * 512 + wc],
                                         start=True, stop=False)
                                self.pe.inc(t.last)
                            qk_pe = self.pe.n

                            @blk.scalar
                            def _(c, c_i=c_i, wc=wc, pslice=pslice, qk_pe=qk_pe,
                                  num_free=num_free):
                                c = Rec(c)
                                c.wait_ge(self.pe.h, qk_pe)
                                c.wait_ge(self.ve.h, num_free)
                                c.activation(self.num_sb[:, c_i * 512: c_i * 512 + wc],
                                             pslice, AF.Square,
                                             scale=sm[:, self.C_SQSCALE: self.C_SQSCALE + 1])
                                self.sc.inc(c.last)
                            sq_scs[c_i] = self.sc.n
                        kc = c_i - 1
                        if kc >= 0:
                            wck = min(512, W - kc * 512)
                            bankk = kc % 3
                            pk = pss[bankk][:, 0:wck]

                            @blk.tensor
                            def _(t, hh=hh, kc=kc, wck=wck, pk=pk, sq_sc=sq_scs[kc]):
                                t = Rec(t)
                                t.wait_ge(self.sc.h, sq_sc)
                                t.matmul(pk, self.ones33[hh * 32: hh * 32 + 1, :],
                                         self.knrowF[hh * 32: hh * 32 + 1,
                                                     kc * 512: kc * 512 + wck],
                                         start=False, stop=True)
                                self.pe.inc(t.last)
                            kn_pes[kc] = self.pe.n

                            @blk.vector
                            def _(v, kc=kc, wck=wck, pk=pk, kn=kn_pes[kc], qn_col=qn_col):
                                v = Rec(v)
                                v.wait_ge(self.pe.h, kn)
                                v.tensor_scalar_add(self.den_sb[:, kc * 512: kc * 512 + wck],
                                                    pk, qn_col)
                                self.ve.inc(v.last)
                            ps_free[kc % 3] = self.ve.n

                    @blk.vector
                    def _(v, W=W, nch=nch, last_sq=sq_scs[nch - 1], den_ve=ps_free[(nch - 1) % 3]):
                        v = Rec(v)
                        v.wait_ge(self.sc.h, last_sq)
                        v.wait_ge(self.ve.h, den_ve)
                        v.tensor_tensor(self.num_sb[:, W - 128: W],
                                        self.num_sb[:, W - 128: W], self.trib[:],
                                        op=ALU.add)
                        self.ve.inc(v.last)
                    mask_ve = self.ve.n

                    @blk.scalar
                    def _(c, W=W, mask_ve=mask_ve):
                        c = Rec(c)
                        c.wait_ge(self.ve.h, mask_ve)
                        self.sc.inc(c.activation(self.rcp_sb[:, 0:W],
                                                 self.den_sb[:, 0:W], AF.Ln))
                        c.wait_ge(self.sc.h, self.sc.n)
                        c.activation(self.rcp_sb[:, 0:W], self.rcp_sb[:, 0:W],
                                     AF.Exp, scale=-1.0)
                        self.sc.inc(c.last)
                    rcp_sc = self.sc.n

                    @blk.vector
                    def _(v, W=W, rcp_sc=rcp_sc):
                        v = Rec(v)
                        v.wait_ge(self.sc.h, rcp_sc)
                        self.ve.inc(v.tensor_tensor(self.den_sb[:, 0:W],
                                                    self.num_sb[:, 0:W],
                                                    self.rcp_sb[:, 0:W], op=ALU.mult))
                        v.wait_ge(self.ve.h, self.ve.n)
                        v.reduce_max(sm[:, self.C_NEGM:self.C_NEGM + 1],
                                     self.den_sb[:, 0:W], axis=AX.X, negate=True)
                        self.ve.inc(v.last)
                    ttr_ve = self.ve.n
                    num_free = ttr_ve

                    @blk.scalar
                    def _(c, W=W, ttr_ve=ttr_ve, e_need=e_free_pe):
                        c = Rec(c)
                        c.wait_ge(self.ve.h, ttr_ve)
                        c.wait_ge(self.pe.h, e_need)
                        c.activation(self.e_sb[:, 0:W],
                                     self.den_sb[:, 0:W], AF.Exp,
                                     bias=sm[:, self.C_NEGM:self.C_NEGM + 1],
                                     accum_out=sm[:, self.C_Z:self.C_Z + 1])
                        self.sc.inc(c.last)
                    exp_sc = self.sc.n

                    @blk.vector
                    def _(v, W=W, exp_sc=exp_sc):
                        v = Rec(v)
                        v.wait_ge(self.sc.h, exp_sc)
                        self.ve.inc(v.reciprocal(sm[:, self.C_ZINV:self.C_ZINV + 1],
                                                 sm[:, self.C_Z:self.C_Z + 1]))
                        v.wait_ge(self.ve.h, self.ve.n)
                        v.tensor_scalar_mul(self.e_sb[:, 0:W], self.e_sb[:, 0:W],
                                            sm[:, self.C_ZINV:self.C_ZINV + 1])
                        self.ve.inc(v.last)
                    enorm_ve = self.ve.n

                    nkt = qt + 1
                    if hh == 0:
                        pav_gate = pav_free_sc
                    for kt in range(nkt):
                        ei = kt % 3
                        sl = kt % 16

                        @blk.tensor
                        def _(t, kt=kt, ei=ei, enorm_ve=enorm_ve,
                              et_need=et_free.get(ei, (None, 0))):
                            t = Rec(t)
                            t.wait_ge(self.ve.h, enorm_ve)
                            eng, cnt = et_need
                            if eng == "v":
                                t.wait_ge(self.ve.h, cnt)
                            elif eng == "c":
                                t.wait_ge(self.sc.h, cnt)
                            t.matmul(pet[ei][:], self.e_sb[:, kt * 128:(kt + 1) * 128],
                                     self.identb[:], start=True, stop=True)
                            self.pe.inc(t.last)
                        tr_pe = self.pe.n

                        if kt % 2 == 0:
                            @blk.vector
                            def _(v, kt=kt, ei=ei, sl=sl, tr_pe=tr_pe,
                                  sf=slot_free.get(sl, 0)):
                                v = Rec(v)
                                v.wait_ge(self.pe.h, max(tr_pe, sf))
                                v.tensor_copy(self.eT_sb[:, sl * 128:(sl + 1) * 128],
                                              pet[ei][:])
                                self.ve.inc(v.last)
                            et_free[ei] = ("v", self.ve.n)
                            evac_eng, evac_cnt = "v", self.ve.n
                        else:
                            @blk.scalar
                            def _(c, kt=kt, ei=ei, sl=sl, tr_pe=tr_pe,
                                  sf=slot_free.get(sl, 0)):
                                c = Rec(c)
                                c.wait_ge(self.pe.h, max(tr_pe, sf))
                                c.copy(self.eT_sb[:, sl * 128:(sl + 1) * 128],
                                       pet[ei][:])
                                self.sc.inc(c.last)
                            et_free[ei] = ("c", self.sc.n)
                            evac_eng, evac_cnt = "c", self.sc.n

                        @blk.tensor
                        def _(t, kt=kt, sl=sl, hh=hh, nkt=nkt,
                              evac_eng=evac_eng, evac_cnt=evac_cnt,
                              pav_gate=(pav_gate if kt == 0 else 0)):
                            t = Rec(t)
                            if evac_eng == "v":
                                t.wait_ge(self.ve.h, evac_cnt)
                            else:
                                t.wait_ge(self.sc.h, evac_cnt)
                            if kt == 0:
                                t.wait_ge(self.sc.h, pav_gate)
                            t.matmul(pav[hh][:],
                                     self.vtm_sb[:, kt * 128 + hh * 64:
                                                 kt * 128 + hh * 64 + 64],
                                     self.eT_sb[:, sl * 128:(sl + 1) * 128],
                                     start=(kt == 0), stop=(kt == nkt - 1))
                            self.pe.inc(t.last)
                        slot_free[sl] = self.pe.n
                        e_free_pe = self.pe.n
                av_pe = self.pe.n

                @blk.scalar
                def _(c, hp=hp, qt=qt, av_pe=av_pe):
                    c = Rec(c)
                    c.wait_ge(self.pe.h, av_pe)
                    self.sc.inc(c.copy(self.av_fm[0:64, hp * 2048 + qt * 128:
                                                  hp * 2048 + (qt + 1) * 128], pav[0][:]))
                    self.sc.inc(c.copy(self.av_fm[64:128, hp * 2048 + qt * 128:
                                                  hp * 2048 + (qt + 1) * 128], pav[1][:]))
                pav_free_sc = self.sc.n
        self.join()

    def emit_wo(self):
        blk = self.block
        nc = self.nc
        pairs_pe = self.pe.n
        w_vals = []

        @blk.gpsimd
        def _(g):
            g.wait_ge(self.pe.h, pairs_pe)
            for dc in range(8):
                wv = self.w.inc(g.dma_start(
                    out=self.wo_sb[:, dc * 1024:(dc + 1) * 1024],
                    in_=self.wop_d[dc]), 16)
                w_vals.append(wv)

        with ExitStack() as pctx:
            pw = [pctx.enter_context(nc.psum_tensor(self.uname("pw"), [128, 1024], F32))
                  for _ in range(2)]
            free_ve = {}
            for tj in range(16):
                half = tj % 2
                need_ve = free_ve.get(half, 0)

                @blk.tensor
                def _(t, tj=tj, half=half, need_ve=need_ve):
                    t = Rec(t)
                    if tj == 0:
                        t.wait_ge(self.w.h, w_vals[-1])
                    t.wait_ge(self.ve.h, need_ve)
                    for ch in range(2):
                        for dc in range(8):
                            lhsT = self.av_fm[:, dc * 2048 + tj * 128: dc * 2048 + (tj + 1) * 128]
                            t.matmul(pw[half][:, ch * 512:(ch + 1) * 512],
                                     lhsT,
                                     self.wo_sb[:, dc * 1024 + ch * 512: dc * 1024 + (ch + 1) * 512],
                                     start=(dc == 0), stop=(dc == 7))
                    self.pe.inc(t.last)
                mm_pe = self.pe.n

                @blk.vector
                def _(v, tj=tj, half=half, mm_pe=mm_pe):
                    v = Rec(v)
                    v.wait_ge(self.pe.h, mm_pe)
                    rsl = self.r[:, tj * 1024:(tj + 1) * 1024]
                    v.tensor_tensor(rsl, rsl, pw[half][:], op=ALU.add)
                    self.ve.inc(v.last)
                free_ve[half] = self.ve.n
        self.join()

    def emit_attention(self):
        blk = self.block
        prev_pe = self.pe.n

        @blk.sync
        def _(s):
            s.wait_ge(self.pe.h, prev_pe)
            self.x.inc(s.dma_start(out=self.c128_s[:], in_=self.c128_d), 16)
            self.x.inc(s.dma_start(out=self.s128_s[:], in_=self.s128_d), 16)
        self.emit_ln(0, 16, self.u_att, 2048, 0)
        for hp in range(8):
            self.att_pair_setup(hp)
            self.att_pair_strips(hp)
        self.emit_wo()

    def emit_store(self):
        blk = self.block
        done_ve = self.ve.n
        done_sc = self.sc.n
        done_pe = self.pe.n

        @blk.sync
        def _(s):
            s.wait_ge(self.ve.h, done_ve)
            s.wait_ge(self.sc.h, done_sc)
            s.wait_ge(self.pe.h, done_pe)
            yout = self.y_d.rearrange("(tt p) d -> p tt d", p=128)
            rin = self.r[:].rearrange("p (tt d) -> p tt d", tt=16)
            self.st.inc(s.dma_start(out=yout, in_=rin), 16)
            s.wait_ge(self.st.h, 16)

    def build(self):
        self.preamble()
        self.emit_ffn(1)
        if self.n_sub >= 2:
            self.emit_attention()
        if self.n_sub >= 3:
            self.emit_ffn(2)
        self.emit_store()


def build_nc(n_sub=3):
    nc = bass.Bass("TRN2", target_bir_lowering=False)
    with ExitStack() as ctx:
        mk = MK(nc, ctx, n_sub=n_sub)
        mk.build()
    return nc


def _blockT(w):
    O, K = w.shape
    t = w.reshape(O // 128, 128, K // 128, 128)
    t = t.transpose(0, 3, 2, 1)
    return np.ascontiguousarray(t.reshape(O // 128, 128, K))


def _blockQ(w):
    t = w.reshape(8, 128, 8, 128)
    t = t.transpose(2, 1, 0, 3)
    return np.ascontiguousarray(t.reshape(8, 128, 1024))


def _pack_inputs(inputs):
    f32 = np.float32
    bf16 = ml_dtypes.bfloat16
    w1 = np.asarray(inputs["L0_ffn_w1_w"], f32)
    w2 = np.asarray(inputs["L0_ffn_w2_w"], f32)
    h1 = np.asarray(inputs["L0_ffn_h1_w"], f32)
    h2 = np.asarray(inputs["L0_ffn_h2_w"], f32)
    w3 = np.asarray(inputs["L0_ffn_w3_w"], f32)
    w1b = _blockT(w1)
    w2b = _blockT(w2)
    w12p = np.ascontiguousarray(np.concatenate([w1b, w2b], axis=2))
    h1p = _blockT(h1)
    h2p = _blockT(h2)
    w3p = np.ascontiguousarray(w3.T).reshape(32, 128, 1024)

    idx = np.concatenate([np.concatenate([np.arange(h * 64, (h + 1) * 64)[0::2],
                                          np.arange(h * 64, (h + 1) * 64)[1::2]])
                          for h in range(NH)])
    wq = np.asarray(inputs["L0_attn_Wq"], f32)[:, idx]
    wk = np.asarray(inputs["L0_attn_Wk"], f32)[:, idx]
    wv = np.asarray(inputs["L0_attn_Wv"], f32)
    wo = np.asarray(inputs["L0_attn_Wo"], f32)

    inv = (1.0 / 10000.0 ** (np.arange(0, HD, 2)[: HD // 2] / HD)).astype(np.float64)
    tpos = np.arange(S_LEN)
    fr = np.outer(tpos, inv)
    c128 = np.ascontiguousarray(np.tile(np.cos(fr).T, (4, 1)).astype(bf16))
    s128 = np.ascontiguousarray(np.tile(np.sin(fr).T, (4, 1)).astype(bf16))

    pmat = np.zeros((128, 128), f32)
    for i in range(128):
        if i % 64 < 32:
            pmat[i + 32, i] = -1.0
        else:
            pmat[i - 32, i] = 1.0

    trib = np.zeros((128, 128), f32)
    trib[np.triu_indices(128, 1)] = -1e30

    oneslo = np.zeros((128, 1), f32)
    oneslo[0:64] = 1.0
    oneshi = np.zeros((128, 1), f32)
    oneshi[64:128] = 1.0

    shared = {
        "w12p": w12p, "h1p": h1p, "h2p": h2p, "w3p": w3p,
        "s1m": np.ascontiguousarray(np.asarray(inputs["L0_ffn_w1_s"], f32).reshape(32, 128).T),
        "s2m": np.ascontiguousarray(np.asarray(inputs["L0_ffn_w2_s"], f32).reshape(32, 128).T),
        "h1m": np.ascontiguousarray(np.asarray(inputs["L0_ffn_h1_s"], f32).reshape(32, 128).T),
        "h2m": np.ascontiguousarray(np.asarray(inputs["L0_ffn_h2_s"], f32).reshape(32, 128).T),
        "s3bch": np.ascontiguousarray(np.broadcast_to(
            np.asarray(inputs["L0_ffn_w3_s"], f32).reshape(1, D), (128, D))),
        "wqp": _blockQ(wq), "wkp": _blockQ(wk), "wvp": _blockQ(wv),
        "wop": np.ascontiguousarray(wo.reshape(8, 128, 1024)),
        "alphacol": np.ascontiguousarray(np.broadcast_to(
            np.asarray(inputs["L0_attn_alpha"], f32).reshape(1, 1), (128, 1))),
        "identb": np.eye(128, dtype=bf16),
        "identf": np.eye(128, dtype=f32),
        "pmat": pmat.astype(bf16), "pmatf": pmat,
        "c128": c128, "s128": s128,
        "trib": trib.astype(bf16),
        "oneslo": oneslo, "oneshi": oneshi,
        "ones33": np.ones((33, 128), f32),
    }
    return shared


def kernel(**inputs):
    x = np.ascontiguousarray(np.asarray(inputs["L0_attn_x"], np.float32))
    shared = _pack_inputs(inputs)
    nc = build_nc(n_sub=3)
    in_maps = []
    for c in range(8):
        m = dict(shared)
        m["x"] = np.ascontiguousarray(x[c])
        in_maps.append(m)
    res = run_bass_kernel_spmd(nc, in_maps, core_ids=list(range(8)))
    out = np.stack([np.asarray(r["y"], np.float32) for r in res.results], axis=0)
    return out
